# version 08
# baseline (speedup 1.0000x reference)
import numpy as np
import ml_dtypes
import concourse.bass as bass
import concourse.mybir as mybir
from concourse.bass_utils import run_bass_kernel_spmd

import contextlib


def _enter(es, cm):
    return es.enter_context(cm)


F32 = mybir.dt.float32
BF16 = mybir.dt.bfloat16
I32 = mybir.dt.int32
AF = mybir.ActivationFunctionType
ALU = mybir.AluOpType
X = mybir.AxisListType.X

SEQ = 8192
D = 2048
H = 8192
HD = 128
R = 16
NCORE = 8
TOWN = SEQ // NCORE
NT = SEQ // 128
NTILES = NT + 8
NCH = SEQ // 512
NQT = 16
EPS = 1e-5
INV_SQRT_HD = float(1.0 / np.sqrt(128.0))

DBG_OUTS = False
PHASES = 6
P1SUB = 3
V_ON = True
NUPP = 128
NDNP = 128


def _cbt(j):
    return 2 * j * (j + 1)


CPH = _cbt(NQT - 1) + 4 * NQT


def _g(h, j, c):
    return CPH * h + _cbt(j) + c


SPH = 16 * 17


def _gs(h, j, s):
    return SPH * h + j * (j + 1) + s


def build_nc():
    nc = bass.Bass("TRN2", target_bir_lowering=False, detect_race_conditions=False)

    x_full = nc.dram_tensor("x_full", [SEQ, D], F32, kind="ExternalInput").ap()
    x_own = nc.dram_tensor("x_own", [TOWN, D], F32, kind="ExternalInput").ap()
    wq_sl = nc.dram_tensor("wq_sl", [D, 256], F32, kind="ExternalInput").ap()
    wk_sl = nc.dram_tensor("wk_sl", [D, 128], F32, kind="ExternalInput").ap()
    wv_sl = nc.dram_tensor("wv_sl", [D, 128], F32, kind="ExternalInput").ap()
    wo = nc.dram_tensor("wo", [D, D], F32, kind="ExternalInput").ap()
    bo_t = nc.dram_tensor("bo_t", [128, 16], F32, kind="ExternalInput").ap()
    wupq = nc.dram_tensor("wupq", [H, D], I32, kind="ExternalInput").ap()
    wups = nc.dram_tensor("wups", [H, 32], F32, kind="ExternalInput").ap()
    bup_t = nc.dram_tensor("bup_t", [128, 64], F32, kind="ExternalInput").ap()
    upla = nc.dram_tensor("upla", [D, R], F32, kind="ExternalInput").ap()
    uplb = nc.dram_tensor("uplb", [R, H], F32, kind="ExternalInput").ap()
    wdnq = nc.dram_tensor("wdnq", [D, H], I32, kind="ExternalInput").ap()
    wdns = nc.dram_tensor("wdns", [D, 128], F32, kind="ExternalInput").ap()
    bdn_t = nc.dram_tensor("bdn_t", [128, 16], F32, kind="ExternalInput").ap()
    dnla = nc.dram_tensor("dnla", [H, R], F32, kind="ExternalInput").ap()
    dnlb = nc.dram_tensor("dnlb", [R, D], F32, kind="ExternalInput").ap()
    masks_in = nc.dram_tensor("masks_in", [128, 4 * 512], BF16, kind="ExternalInput").ap()
    onesc_in = nc.dram_tensor("onesc_in", [128, 1], BF16, kind="ExternalInput").ap()
    onesr_in = nc.dram_tensor("onesr_in", [1, 128], F32, kind="ExternalInput").ap()
    tokoff_in = nc.dram_tensor("tokoff_in", [1, 1], I32, kind="ExternalInput").ap()
    out = nc.dram_tensor("out", [TOWN, D], F32, kind="ExternalOutput").ap()

    _ik = "ExternalOutput" if DBG_OUTS else "Internal"
    dequp = nc.dram_tensor("dequp", [H, D], BF16, kind=_ik)
    deqdn = nc.dram_tensor("deqdn", [D, H], BF16, kind=_ik)
    q1_hbm = nc.dram_tensor("q1_hbm", [128, SEQ], BF16, kind=_ik)
    xn_hbm = nc.dram_tensor("xn_hbm", [NTILES * 128, D], BF16, kind=_ik)
    ctx_in = nc.dram_tensor("ctx_in", [256, SEQ], BF16)
    ctx_out = nc.dram_tensor("ctx_out", [D, SEQ], BF16)
    s_hbm = nc.dram_tensor("s_hbm", [D, TOWN], BF16, kind=_ik)
    sg_hbm = nc.dram_tensor("sg_hbm", [D, TOWN], BF16, kind=_ik)
    if DBG_OUTS:
        ctx_dbg = nc.dram_tensor("ctx_dbg", [256, SEQ], BF16, kind="ExternalOutput")
        ctxo_dbg = nc.dram_tensor("ctxo_dbg", [D, SEQ], BF16, kind="ExternalOutput")

    ctx_out_re = ctx_out.ap().rearrange("(c p) t -> p c t", p=128)

    with contextlib.ExitStack() as _es4:
        s1a = _enter(_es4, nc.sbuf_tensor("s1a", [128, NTILES], F32))
        s2a = _enter(_es4, nc.sbuf_tensor("s2a", [128, NTILES], F32))
        sm1 = _enter(_es4, nc.sbuf_tensor("sm1", [128, NTILES], F32))
        rstda = _enter(_es4, nc.sbuf_tensor("rstda", [128, NTILES], F32))
        nba = _enter(_es4, nc.sbuf_tensor("nba", [128, NTILES], F32))
        q2m = _enter(_es4, nc.sbuf_tensor("q2m", [1, 64], F32))
        mrow = _enter(_es4, nc.sbuf_tensor("mrow", [1, 64], F32))
        negm = _enter(_es4, nc.sbuf_tensor("negm", [128, 32], F32))
        onesc = _enter(_es4, nc.sbuf_tensor("onesc", [128, 1], BF16))
        onesr = _enter(_es4, nc.sbuf_tensor("onesr", [1, 128], F32))
        tokoff_sb = _enter(_es4, nc.sbuf_tensor("tokoff", [1, 1], I32))
        bo_sb = _enter(_es4, nc.sbuf_tensor("bo_sb", [128, 16], F32))
        bup_sb = _enter(_es4, nc.sbuf_tensor("bup_sb", [128, 64], F32))
        bdn_sb = _enter(_es4, nc.sbuf_tensor("bdn_sb", [128, 16], F32))
        dnla_sb = _enter(_es4, nc.sbuf_tensor("dnla_sb", [128, 64, R], BF16))
        t2_sb = _enter(_es4, nc.sbuf_tensor("t2_sb", [16, TOWN], BF16))
        ps_dmy = _enter(_es4, nc.psum_tensor("ps_dmy", [128, 512], F32))
        sv1 = _enter(_es4, nc.sbuf_tensor("sv1", [128, NTILES], F32))
        mrow2 = _enter(_es4, nc.sbuf_tensor("mrow2", [1, 64], F32))
        dw = _enter(_es4, nc.semaphore("dw"))
        dx = _enter(_es4, nc.semaphore("dx"))
        dtr = _enter(_es4, nc.semaphore("dtr"))
        dvt = _enter(_es4, nc.semaphore("dvt"))
        dxn = _enter(_es4, nc.semaphore("dxn"))
        dxb = _enter(_es4, nc.semaphore("dxb"))
        s_nm = _enter(_es4, nc.semaphore("s_nm"))
        t_half = _enter(_es4, nc.semaphore("t_half"))
        dqo = _enter(_es4, nc.semaphore("dqo"))
        dql = _enter(_es4, nc.semaphore("dql"))
        dctx = _enter(_es4, nc.semaphore("dctx"))
        cc = _enter(_es4, nc.semaphore("cc"))
        dcf = _enter(_es4, nc.semaphore("dcf"))
        dwo = _enter(_es4, nc.semaphore("dwo"))
        dse = _enter(_es4, nc.semaphore("dse"))
        dst = _enter(_es4, nc.semaphore("dst"))
        dub = _enter(_es4, nc.semaphore("dub"))
        ddb = _enter(_es4, nc.semaphore("ddb"))
        dtru = _enter(_es4, nc.semaphore("dtru"))
        dtrd = _enter(_es4, nc.semaphore("dtrd"))
        dpu_i = _enter(_es4, nc.semaphore("dpu_i"))
        dpu_o = _enter(_es4, nc.semaphore("dpu_o"))
        dpd_i = _enter(_es4, nc.semaphore("dpd_i"))
        dpd_o = _enter(_es4, nc.semaphore("dpd_o"))
        dsg = _enter(_es4, nc.semaphore("dsg"))
        dtp = _enter(_es4, nc.semaphore("dtp"))
        dxo = _enter(_es4, nc.semaphore("dxo"))
        dout = _enter(_es4, nc.semaphore("dout"))
        sdbg = _enter(_es4, nc.semaphore("sdbg"))
        s_stat = _enter(_es4, nc.semaphore("s_stat"))
        v_stat = _enter(_es4, nc.semaphore("v_stat"))
        v_small = _enter(_es4, nc.semaphore("v_small"))
        s_sq = _enter(_es4, nc.semaphore("s_sq"))
        t_pre = _enter(_es4, nc.semaphore("t_pre"))
        v_q2m = _enter(_es4, nc.semaphore("v_q2m"))
        t_bk = _enter(_es4, nc.semaphore("t_bk"))
        s_ev = _enter(_es4, nc.semaphore("s_ev"))
        v_ev = _enter(_es4, nc.semaphore("v_ev"))
        t_sc = _enter(_es4, nc.semaphore("t_sc"))
        s_exp = _enter(_es4, nc.semaphore("s_exp"))
        v_mask = _enter(_es4, nc.semaphore("v_mask"))
        t_pv = _enter(_es4, nc.semaphore("t_pv"))
        t_sum = _enter(_es4, nc.semaphore("t_sum"))
        v_rec = _enter(_es4, nc.semaphore("v_rec"))
        t_rbc = _enter(_es4, nc.semaphore("t_rbc"))
        v_ctx = _enter(_es4, nc.semaphore("v_ctx"))
        s_pre = _enter(_es4, nc.semaphore("s_pre"))
        v_pre = _enter(_es4, nc.semaphore("v_pre"))
        t_negm = _enter(_es4, nc.semaphore("t_negm"))
        v_dqu = _enter(_es4, nc.semaphore("v_dqu"))
        v_dqd = _enter(_es4, nc.semaphore("v_dqd"))
        p1 = _enter(_es4, nc.semaphore("p1"))
        p2 = _enter(_es4, nc.semaphore("p2"))
        p3 = _enter(_es4, nc.semaphore("p3"))
        p4 = _enter(_es4, nc.semaphore("p4"))
        p5 = _enter(_es4, nc.semaphore("p5"))
        s_t1 = _enter(_es4, nc.semaphore("s_t1"))
        s_t2 = _enter(_es4, nc.semaphore("s_t2"))
        t_t1 = _enter(_es4, nc.semaphore("t_t1"))
        t_t2 = _enter(_es4, nc.semaphore("t_t2"))
        t_y1 = _enter(_es4, nc.semaphore("t_y1"))
        s_x2 = _enter(_es4, nc.semaphore("s_x2"))
        t_s = _enter(_es4, nc.semaphore("t_s"))
        s_sev = _enter(_es4, nc.semaphore("s_sev"))
        t_y2 = _enter(_es4, nc.semaphore("t_y2"))
        s_g = _enter(_es4, nc.semaphore("s_g"))
        v_sg = _enter(_es4, nc.semaphore("v_sg"))
        v_out = _enter(_es4, nc.semaphore("v_out"))
        block = _enter(_es4, nc.Block())
        with contextlib.ExitStack() as _es8:
            K_fm = _enter(_es8, nc.sbuf_tensor("K_fm", [128, SEQ], BF16))
            V_tm = _enter(_es8, nc.sbuf_tensor("V_tm", [128, 64, 128], BF16))
            Q0 = _enter(_es8, nc.sbuf_tensor("Q0", [128, SEQ], BF16))
            pint = _enter(_es8, nc.sbuf_tensor("pint", [128, 2, 1024], I32))
            wdns_sb = _enter(_es8, nc.sbuf_tensor("wdns_sb", [128, 16, 128], F32))
            wups_sb = _enter(_es8, nc.sbuf_tensor("wups_sb", [128, 64, 32], F32))
            rrow = _enter(_es8, nc.sbuf_tensor("rrow", [1, 512], F32))
            rbc_sb = _enter(_es8, nc.sbuf_tensor("rbc_sb", [128, 512], F32))
            pdeq = _enter(_es8, nc.sbuf_tensor("pdeq", [128, 2, 1024], BF16))
            with contextlib.ExitStack() as _es12:
                wq_bf = _enter(_es12, nc.sbuf_tensor("wq_bf", [128, 16, 256], BF16))
                sqscr = _enter(_es12, nc.sbuf_tensor("sqscr", [128, 2048], BF16))
                q2scr = _enter(_es12, nc.sbuf_tensor("q2scr", [128, 1024], BF16))
                wk_bf = _enter(_es12, nc.sbuf_tensor("wk_bf", [128, 16, 128], BF16))
                wv_bf = _enter(_es12, nc.sbuf_tensor("wv_bf", [128, 16, 128], BF16))
                xbuf = _enter(_es12, nc.sbuf_tensor("xbuf", [128, 3, 2048], F32))
                xn_tm = _enter(_es12, nc.sbuf_tensor("xn_tm", [128, 2, 2048], BF16))
                xnf = _enter(_es12, nc.sbuf_tensor("xnf", [128, 16, 512], BF16))
                q1_st = _enter(_es12, nc.sbuf_tensor("q1_st", [128, 2, 512], BF16))
                ps_q0 = _enter(_es12, nc.psum_tensor("ps_q0", [128, 512], F32))
                ps_q1 = _enter(_es12, nc.psum_tensor("ps_q1", [128, 512], F32))
                ps_k = _enter(_es12, nc.psum_tensor("ps_k", [128, 512], F32))
                ps_v = _enter(_es12, nc.psum_tensor("ps_v", [128, 512], F32))
                ps_pre = _enter(_es12, nc.psum_tensor("ps_pre", [128, 1024], F32))
                def tile_src(t):
                    if t < NT:
                        return x_full[128 * t:128 * (t + 1), :]
                    tt = t - NT
                    return x_own[128 * tt:128 * (tt + 1), :]

                @block.gpsimd
                def _(g):
                    g.dma_start(out=tokoff_sb[:, :], in_=tokoff_in).then_inc(dw, 16)
                    g.dma_start(out=onesc[:, :], in_=onesc_in).then_inc(dw, 16)
                    g.dma_start(out=onesr[:, :], in_=onesr_in).then_inc(dw, 16)
                    g.dma_start(out=wq_bf[:, :, :],
                                in_=wq_sl.rearrange("(c p) o -> p c o", p=128)).then_inc(dw, 16)
                    g.dma_start(out=wk_bf[:, :, :],
                                in_=wk_sl.rearrange("(c p) o -> p c o", p=128)).then_inc(dw, 16)
                    g.dma_start(out=wv_bf[:, :, :],
                                in_=wv_sl.rearrange("(c p) o -> p c o", p=128)).then_inc(dw, 16)
                    g.dma_start(out=wdns_sb[:, :, :],
                                in_=wdns.rearrange("(c p) b -> p c b", p=128)).then_inc(dw, 16)
                    g.dma_start(out=wups_sb[:, :, :],
                                in_=wups.rearrange("(c p) b -> p c b", p=128)).then_inc(dw, 16)
                    g.dma_start(out=bo_sb[:, :], in_=bo_t).then_inc(dw, 16)
                    g.dma_start(out=bup_sb[:, :], in_=bup_t).then_inc(dw, 16)
                    g.dma_start(out=bdn_sb[:, :], in_=bdn_t).then_inc(dw, 16)
                    g.dma_start(out=dnla_sb[:, :, :],
                                in_=dnla.rearrange("(c p) r -> p c r", p=128)).then_inc(dw, 16)
                    for t in range(NTILES if P1SUB >= 1 else 0):
                        g.wait_ge(s_nm, t + 1)
                        g.dma_start(out=xn_hbm.ap()[128 * t:128 * (t + 1), :],
                                    in_=xn_tm[:, t % 2, :]).then_inc(dxn, 16)
                        if P1SUB >= 2 and t < NT and t % 4 == 3:
                            u = t // 4
                            g.wait_ge(v_ev, 2 * u + 1)
                            g.dma_start(out=q1_hbm.ap()[:, 512 * u:512 * (u + 1)],
                                        in_=q1_st[:, u % 2, :]).then_inc(dqo, 16)
                    g.wait_ge(dxn, 16 * NTILES)
                    g.wait_ge(dqo, 16 * NCH)
                    g.nop().then_inc(p1, 1)

                @block.sync
                def _(sy):
                    for t in range(NTILES):
                        if t >= 3:
                            sy.wait_ge(s_stat, t - 2)
                            sy.wait_ge(v_stat, t - 2)
                            if P1SUB >= 1:
                                sy.wait_ge(s_nm, t - 2)
                        sy.dma_start(out=xbuf[:, t % 3, :], in_=tile_src(t)).then_inc(dx, 16)
                        tt = t - 2
                        if P1SUB >= 1 and 0 <= tt < NT:
                            u2, ti2 = tt // 4, tt % 4
                            if ti2 == 0 and u2 >= 1 and P1SUB >= 2:
                                sy.wait_ge(t_bk, 4 * u2)
                            sy.wait_ge(dxn, 16 * (tt + 1))
                            sy.dma_start(out=xnf[:, :, 128 * ti2:128 * (ti2 + 1)],
                                         in_=xn_hbm.ap()[128 * tt:128 * (tt + 1), :],
                                         transpose=True).then_inc(dtr, 16)
                    for tt in range(NT - 2, NT):
                        if P1SUB >= 1:
                            sy.wait_ge(dxn, 16 * (tt + 1))
                            sy.dma_start(out=xnf[:, :, 128 * (tt % 4):128 * (tt % 4 + 1)],
                                         in_=xn_hbm.ap()[128 * tt:128 * (tt + 1), :],
                                         transpose=True).then_inc(dtr, 16)
                    for p in range(NUPP if PHASES >= 2 else 0):
                        hc, half = p // 2, p % 2
                        if p >= 2:
                            sy.wait_ge(v_dqu, p - 1)
                        sy.dma_start(
                            out=pint[:, p % 2, :],
                            in_=wupq[128 * hc:128 * (hc + 1), 1024 * half:1024 * (half + 1)],
                        ).then_inc(dpu_i, 16)
                        sy.wait_ge(v_dqu, p + 1)
                        sy.dma_start(
                            out=dequp.ap()[128 * hc:128 * (hc + 1), 1024 * half:1024 * (half + 1)],
                            in_=pdeq[:, p % 2, :],
                        ).then_inc(dpu_o, 16)
                    for p in range(NDNP if PHASES >= 2 else 0):
                        dt, hp = p // 8, p % 8
                        if p >= 2:
                            sy.wait_ge(v_dqd, p - 1)
                        sy.dma_start(
                            out=pint[:, p % 2, :],
                            in_=wdnq[128 * dt:128 * (dt + 1), 1024 * hp:1024 * (hp + 1)],
                        ).then_inc(dpd_i, 16)
                        sy.wait_ge(v_dqd, p + 1)
                        sy.dma_start(
                            out=deqdn.ap()[128 * dt:128 * (dt + 1), 1024 * hp:1024 * (hp + 1)],
                            in_=pdeq[:, p % 2, :],
                        ).then_inc(dpd_o, 16)

                @block.scalar
                def _(s):
                    for t in range(NTILES):
                        u, ti = t // 4, t % 4
                        s.wait_ge(dx, 16 * (t + 1))
                        s.activation(sqscr[:, :], xbuf[:, t % 3, :], AF.Square,
                                     accum_out=s2a[:, t:t + 1]).then_inc(s_stat, 1)
                        s.wait_ge(v_small, t + 1)
                        s.activation(sv1[:, t:t + 1], sm1[:, t:t + 1], AF.Ln)
                        s.drain()
                        s.activation(rstda[:, t:t + 1], sv1[:, t:t + 1], AF.Exp, scale=-0.5)
                        s.drain()
                        s.mul(sm1[:, t:t + 1], rstda[:, t:t + 1], nba[:, t:t + 1])
                        s.drain()
                        if P1SUB >= 1:
                            if t >= 2:
                                s.wait_ge(dxn, 16 * (t - 1))
                            s.activation(xn_tm[:, t % 2, :], xbuf[:, t % 3, :], AF.Identity,
                                         bias=sm1[:, t:t + 1],
                                         scale=rstda[:, t:t + 1]).then_inc(s_nm, 1)
                        if P1SUB >= 2 and t < NT and ti == 3:
                            s.wait_ge(t_bk, 4 * u + 1)
                            s.activation(Q0[:, 512 * u:512 * (u + 1)], ps_q0[:, :], AF.Copy,
                                         scale=INV_SQRT_HD).then_inc(s_ev, 1)
                            s.wait_ge(t_bk, 4 * u + 4)
                            s.activation(V_tm[:, 4 * u:4 * u + 4, :],
                                         ps_v[:, :].rearrange("p (a b) -> p a b", b=128),
                                         AF.Copy).then_inc(s_ev, 1)
                            if P1SUB < 3:
                                continue
                            i0 = 3 * u
                            if i0 >= 2:
                                s.wait_ge(t_pre, i0 - 1)
                            s.activation(q2scr[:, 512 * (i0 % 2):512 * (i0 % 2) + 512],
                                         Q0[:, 512 * u:512 * (u + 1)], AF.Square).then_inc(s_sq, 1)
                            i1 = i0 + 1
                            s.wait_ge(v_ev, 2 * u + 1)
                            if i1 >= 2:
                                s.wait_ge(t_pre, i1 - 1)
                            s.activation(q2scr[:, 512 * (i1 % 2):512 * (i1 % 2) + 512],
                                         q1_st[:, u % 2, :], AF.Square).then_inc(s_sq, 1)
                            i2 = i0 + 2
                            s.wait_ge(v_ev, 2 * u + 2)
                            if i2 >= 2:
                                s.wait_ge(t_pre, i2 - 1)
                            s.activation(q2scr[:, 512 * (i2 % 2):512 * (i2 % 2) + 512],
                                         K_fm[:, 512 * u:512 * (u + 1)], AF.Square).then_inc(s_sq, 1)
                    s.copy(sv1[:, 0:1], rstda[:, 0:1]).then_inc(p1, 1)

                @block.vector
                def _(v):
                    for t in range(NTILES):
                        u, ti = t // 4, t % 4
                        v.wait_ge(dx, 16 * (t + 1))
                        v.reduce_sum(s1a[:, t:t + 1], xbuf[:, t % 3, :], axis=X).then_inc(v_stat, 1)
                        v.wait_ge(s_stat, t + 1)
                        v.tensor_scalar_mul(nba[:, t:t + 1], s1a[:, t:t + 1], -1.0 / D)
                        v.tensor_scalar_mul(sv1[:, t:t + 1], s1a[:, t:t + 1], 1.0 / D)
                        v.drain()
                        v.tensor_mul(sm1[:, t:t + 1], sv1[:, t:t + 1], nba[:, t:t + 1])
                        v.drain()
                        v.scalar_tensor_tensor(
                            out=sv1[:, t:t + 1], in0=s2a[:, t:t + 1], scalar=1.0 / D,
                            in1=sm1[:, t:t + 1], op0=ALU.mult, op1=ALU.add,
                        )
                        v.drain()
                        v.tensor_scalar_add(sm1[:, t:t + 1], sv1[:, t:t + 1],
                                            EPS).then_inc(v_small, 1)
                        if P1SUB >= 2 and t < NT and ti == 3:
                            v.wait_ge(t_bk, 4 * u + 2)
                            if u >= 2:
                                v.wait_ge(dqo, 16 * (u - 1))
                            v.tensor_scalar_mul(q1_st[:, u % 2, :], ps_q1[:, :],
                                                INV_SQRT_HD).then_inc(v_ev, 1)
                            v.wait_ge(t_bk, 4 * u + 3)
                            v.tensor_copy(K_fm[:, 512 * u:512 * (u + 1)],
                                          ps_k[:, :]).then_inc(v_ev, 1)
                            for m in range(3 if P1SUB >= 3 else 0):
                                i = 3 * u + m
                                v.wait_ge(t_pre, i + 1)
                                col = m * 16 + u
                                v.reduce_max(q2m[0:1, col:col + 1],
                                             ps_pre[0:1, 512 * (i % 2):512 * (i % 2) + 512],
                                             axis=X).then_inc(v_q2m, 1)
                    v.tensor_copy(sv1[:, 1:2], rstda[:, 1:2]).then_inc(p1, 1)

                @block.tensor
                def _(tn):
                    tn.wait_ge(dw, 16 * 12)
                    for u in range(NCH if P1SUB >= 2 else 0):
                        tn.wait_ge(dtr, 16 * (4 * u + 4))
                        if u >= 1:
                            tn.wait_ge(s_ev, 2 * u - 1)
                        for dcx in range(16):
                            mm = tn.matmul(ps_q0[:, :], wq_bf[:, dcx, 0:128], xnf[:, dcx, :],
                                           start=(dcx == 0), stop=(dcx == 15))
                        mm.then_inc(t_bk, 1)
                        if u >= 1:
                            tn.wait_ge(v_ev, 2 * u - 1)
                        for dcx in range(16):
                            mm = tn.matmul(ps_q1[:, :], wq_bf[:, dcx, 128:256], xnf[:, dcx, :],
                                           start=(dcx == 0), stop=(dcx == 15))
                        mm.then_inc(t_bk, 1)
                        if u >= 1:
                            tn.wait_ge(v_ev, 2 * u)
                        for dcx in range(16):
                            mm = tn.matmul(ps_k[:, :], wk_bf[:, dcx, :], xnf[:, dcx, :],
                                           start=(dcx == 0), stop=(dcx == 15))
                        mm.then_inc(t_bk, 1)
                        if u >= 1:
                            tn.wait_ge(s_ev, 2 * u)
                        for ti in range(4):
                            for dcx in range(16):
                                mm = tn.matmul(ps_v[:, 128 * ti:128 * (ti + 1)],
                                               xnf[:, dcx, 128 * ti:128 * (ti + 1)],
                                               wv_bf[:, dcx, :],
                                               start=(dcx == 0), stop=(dcx == 15))
                        mm.then_inc(t_bk, 1)
                        for m in range(3 if P1SUB >= 3 else 0):
                            i = 3 * u + m
                            tn.wait_ge(s_sq, i + 1)
                            if i >= 2:
                                tn.wait_ge(v_q2m, i - 1)
                            tn.matmul(ps_pre[0:1, 512 * (i % 2):512 * (i % 2) + 512],
                                      onesc[:, :],
                                      q2scr[:, 512 * (i % 2):512 * (i % 2) + 512],
                                      start=True, stop=True).then_inc(t_pre, 1)
                    tn.matmul(ps_dmy[0:1, 0:1], onesc[0:1, 0:1], onesc[0:1, 0:1],
                              start=True, stop=True).then_inc(p1, 1)

            NDIAG_H = 64
            with contextlib.ExitStack() as _es12:
                masks = _enter(_es12, nc.sbuf_tensor("masks", [128, 4, 512], BF16))
                p_sb = _enter(_es12, nc.sbuf_tensor("p_sb", [128, 2, 1024], BF16))
                qld = _enter(_es12, nc.sbuf_tensor("qld", [128, 2, 512], BF16))
                ctx_ev = _enter(_es12, nc.sbuf_tensor("ctx_ev", [128, 2, 512], BF16))
                ps_sc = _enter(_es12, nc.psum_tensor("ps_sc", [128, 2048], F32))
                ps_ctx = _enter(_es12, nc.psum_tensor("ps_ctx", [128, 512], F32))
                ps_sum = _enter(_es12, nc.psum_tensor("ps_sum", [128, 512], F32))
                ps_rbc = _enter(_es12, nc.psum_tensor("ps_rbc", [128, 512], F32))
                @block.gpsimd
                def _(g):
                    if PHASES < 2:
                        return
                    g.wait_ge(p1, 4)
                    g.dma_start(out=masks[:, :, :],
                                in_=masks_in.rearrange("p (r n) -> p r n", n=512)).then_inc(dw, 16)
                    g.dma_start(out=qld[:, 0, :],
                                in_=q1_hbm.ap()[:, 0:512]).then_inc(dql, 16)
                    g.dma_start(out=qld[:, 1, :],
                                in_=q1_hbm.ap()[:, 512:1024]).then_inc(dql, 16)
                    for j in range(NQT):
                        g.wait_ge(v_ctx, j + 1)
                        g.dma_start(
                            out=ctx_in.ap()[0:128, 512 * j:512 * (j + 1)],
                            in_=ctx_ev[:, j % 2, :],
                        ).then_inc(dctx, 16)
                    for j in range(2, NQT):
                        g.wait_ge(t_sum, 15 + j)
                        g.dma_start(out=qld[:, j % 2, :],
                                    in_=q1_hbm.ap()[:, 512 * j:512 * (j + 1)]).then_inc(dql, 16)
                        jj = j - 2
                        g.wait_ge(v_ctx, 17 + jj)
                        g.dma_start(
                            out=ctx_in.ap()[128:256, 512 * jj:512 * (jj + 1)],
                            in_=ctx_ev[:, (16 + jj) % 2, :],
                        ).then_inc(dctx, 16)
                    for jj in (14, 15):
                        g.wait_ge(v_ctx, 17 + jj)
                        g.dma_start(
                            out=ctx_in.ap()[128:256, 512 * jj:512 * (jj + 1)],
                            in_=ctx_ev[:, (16 + jj) % 2, :],
                        ).then_inc(dctx, 16)
                    g.wait_ge(dctx, 16 * 32)
                    if DBG_OUTS:
                        g.dma_start(out=ctx_dbg.ap(), in_=ctx_in.ap()).then_inc(dw, 16)
                    g.collective_compute(
                        "AllGather", ALU.bypass,
                        replica_groups=[list(range(NCORE))],
                        ins=[ctx_in.ap().opt()],
                        outs=[ctx_out.ap().opt()],
                    ).then_inc(cc, 1)
                    g.wait_ge(dql, 16 * NQT)
                    if DBG_OUTS:
                        g.wait_ge(cc, 1)
                        g.dma_start(out=ctxo_dbg.ap(), in_=ctx_out.ap()).then_inc(dw, 16)
                        g.wait_ge(dw, 16 * 15)
                    g.nop().then_inc(p2, 1)

                @block.scalar
                def _(s):
                    if PHASES < 2:
                        return
                    s.wait_ge(p1, 4)
                    for h in range(2):
                        base = 16 * h
                        s.wait_ge(v_q2m, 48)
                        s.wait_ge(v_pre, 2 * h + 1)
                        s.activation(mrow[0:1, 0:16], q2m[0:1, base:base + 16], AF.Ln)
                        s.drain()
                        s.activation(mrow2[0:1, 0:1], q2m[0:1, 48:49], AF.Ln).then_inc(s_pre, 1)
                        s.wait_ge(v_pre, 2 * h + 2)
                        if h == 1:
                            s.wait_ge(t_negm, 1)
                        s.activation(mrow2[0:1, 16:32], mrow[0:1, 48:64], AF.Exp)
                        s.drain()
                        s.mul(mrow[0:1, 16:32], mrow2[0:1, 16:32], -1.05).then_inc(s_pre, 1)
                        s.wait_ge(t_negm, h + 1)
                        s.copy(negm[:, base:base + 16], ps_rbc[:, 0:16]).then_inc(s_pre, 1)
                        s.drain()
                        for j in range(NQT):
                            nsc = 2 * (j + 1)
                            for sc in range(nsc):
                                g = _gs(h, j, sc)
                                s.wait_ge(t_sc, g + 1)
                                if g >= 2:
                                    s.wait_ge(t_pv, g - 1)
                                s.activation(
                                    p_sb[:, g % 2, :],
                                    ps_sc[:, 1024 * (g % 2):1024 * (g % 2) + 1024],
                                    AF.Exp,
                                    bias=negm[:, base + j:base + j + 1],
                                ).then_inc(s_exp, 1)
                    s.copy(sv1[:, 2:3], rstda[:, 2:3]).then_inc(p2, 1)

                @block.vector
                def _(v):
                    if PHASES < 2:
                        return
                    v.wait_ge(p1, 4)
                    ndq = [0]

                    def emit_dq(vv, n):
                        for _ in range(n):
                            p = ndq[0]
                            if p < NUPP:
                                hc, half = p // 2, p % 2
                                vv.wait_ge(dpu_i, 16 * (p + 1))
                                if p >= 2:
                                    vv.wait_ge(dpu_o, 16 * (p - 1))
                                scale = wups_sb[:, hc, 16 * half:16 * half + 16]
                                vv.scalar_tensor_tensor(
                                    out=pdeq[:, p % 2, :].rearrange("p (b c) -> p b c", c=64),
                                    in0=pint[:, p % 2, :].rearrange("p (b c) -> p b c", c=64),
                                    scalar=-7.5,
                                    in1=scale.unsqueeze(2).broadcast_to([128, 16, 64]),
                                    op0=ALU.add, op1=ALU.mult,
                                ).then_inc(v_dqu, 1)
                            elif p < NUPP + NDNP:
                                pd = p - NUPP
                                dt, hp = pd // 8, pd % 8
                                vv.wait_ge(dpd_i, 16 * (pd + 1))
                                if pd >= 2:
                                    vv.wait_ge(dpd_o, 16 * (pd - 1))
                                scale = wdns_sb[:, dt, 16 * hp:16 * hp + 16]
                                vv.scalar_tensor_tensor(
                                    out=pdeq[:, pd % 2, :].rearrange("p (b c) -> p b c", c=64),
                                    in0=pint[:, pd % 2, :].rearrange("p (b c) -> p b c", c=64),
                                    scalar=-7.5,
                                    in1=scale.unsqueeze(2).broadcast_to([128, 16, 64]),
                                    op0=ALU.add, op1=ALU.mult,
                                ).then_inc(v_dqd, 1)
                            else:
                                return
                            ndq[0] += 1

                    for h in range(2):
                        if h == 0:
                            v.reduce_max(q2m[0:1, 48:49], q2m[0:1, 32:48],
                                         axis=X).then_inc(v_pre, 1)
                        else:
                            v.nop().then_inc(v_pre, 1)
                        v.wait_ge(s_pre, 3 * h + 1)
                        v.tensor_scalar(
                            out=mrow[0:1, 48:64], in0=mrow[0:1, 0:16],
                            scalar1=mrow2[0:1, 0:1], scalar2=0.5,
                            op0=ALU.add, op1=ALU.mult,
                        ).then_inc(v_pre, 1)
                        for j in range(NQT):
                            ti = 16 * h + j
                            nsc = 2 * (j + 1)
                            for sc in range(nsc - 2, nsc):
                                g = _gs(h, j, sc)
                                v.wait_ge(s_exp, g + 1)
                                r2 = sc - (nsc - 2)
                                v.tensor_mul(
                                    p_sb[:, g % 2, :].rearrange("p (a b) -> p a b", b=512),
                                    p_sb[:, g % 2, :].rearrange("p (a b) -> p a b", b=512),
                                    masks[:, 2 * r2:2 * r2 + 2, :]).then_inc(v_mask, 1)
                            v.wait_ge(t_sum, ti + 1)
                            v.reciprocal(rrow[0:1, :], ps_sum[0:1, :]).then_inc(v_rec, 1)
                            v.wait_ge(t_rbc, ti + 1)
                            v.tensor_copy(rbc_sb[:, :], ps_rbc[:, :])
                            v.drain()
                            if ti >= 2:
                                v.wait_ge(dctx, 16 * (ti - 1))
                            v.tensor_mul(ctx_ev[:, ti % 2, :], ps_ctx[:, :],
                                         rbc_sb[:, :]).then_inc(v_ctx, 1)
                            emit_dq(v, 8)
                    emit_dq(v, NUPP + NDNP - ndq[0])
                    v.tensor_copy(sv1[:, 3:4], rstda[:, 3:4]).then_inc(p2, 1)

                def _emit_sum_pv(tn, h, j, sc, nsc):
                    g = _gs(h, j, sc)
                    ti = 16 * h + j
                    if sc >= nsc - 2:
                        di = 32 * h + 2 * j + (sc - (nsc - 2))
                        tn.wait_ge(v_mask, di + 1)
                    else:
                        tn.wait_ge(s_exp, g + 1)
                    if sc == 0:
                        tn.wait_ge(v_rec, ti)
                        tn.wait_ge(v_ctx, ti)
                    for half in range(2):
                        first = (sc == 0 and half == 0)
                        last = (sc == nsc - 1 and half == 1)
                        mm = tn.matmul(ps_sum[0:1, :], onesc[:, :],
                                       p_sb[:, g % 2, 512 * half:512 * (half + 1)],
                                       start=first, stop=last)
                        if last:
                            mm.then_inc(t_sum, 1)
                        mm2 = tn.matmul(ps_ctx[:, :], V_tm[:, 2 * sc + half, :],
                                        p_sb[:, g % 2, 512 * half:512 * (half + 1)],
                                        start=first, stop=last)
                    mm2.then_inc(t_pv, 1)

                @block.tensor
                def _(tn):
                    if PHASES < 2:
                        return
                    tn.wait_ge(p1, 4)
                    tn.wait_ge(dw, 16 * 13)
                    for h in range(2):
                        tn.wait_ge(s_pre, 3 * h + 2)
                        if h == 1:
                            tn.wait_ge(v_ctx, 16)
                        for j in range(NQT):
                            mm = tn.matmul(ps_rbc[:, j:j + 1], onesr[:, :],
                                           mrow[0:1, 16 + j:17 + j], start=True, stop=True)
                        mm.then_inc(t_negm, 1)
                        for j in range(NQT):
                            ti = 16 * h + j
                            nsc = 2 * (j + 1)
                            if h == 1:
                                tn.wait_ge(dql, 16 * (j + 1))
                            qrhs = (Q0[:, 512 * j:512 * (j + 1)] if h == 0
                                    else qld[:, j % 2, :])
                            for sc in range(nsc):
                                g = _gs(h, j, sc)
                                if g >= 2:
                                    tn.wait_ge(s_exp, g - 1)
                                if sc == 0:
                                    tn.wait_ge(s_pre, 3 * h + 3)
                                for half in range(2):
                                    c = 2 * sc + half
                                    mm = tn.matmul(
                                        ps_sc[:, 1024 * (g % 2) + 512 * half:
                                              1024 * (g % 2) + 512 * (half + 1)],
                                        K_fm[:, 128 * c:128 * (c + 1)], qrhs,
                                        start=True, stop=True)
                                mm.then_inc(t_sc, 1)
                                if sc >= 1:
                                    _emit_sum_pv(tn, h, j, sc - 1, nsc)
                            _emit_sum_pv(tn, h, j, nsc - 1, nsc)
                            tn.wait_ge(v_rec, ti + 1)
                            tn.matmul(ps_rbc[:, :], onesr[:, :], rrow[0:1, :],
                                      start=True, stop=True).then_inc(t_rbc, 1)
                    tn.matmul(ps_dmy[0:1, 0:1], onesc[0:1, 0:1], onesc[0:1, 0:1],
                              start=True, stop=True).then_inc(p2, 1)

        with contextlib.ExitStack() as _es8:
            x2 = _enter(_es8, nc.sbuf_tensor("x2", [128, 64, TOWN], BF16))
            ps_y = _enter(_es8, nc.psum_tensor("ps_y", [128, 2048], F32))
            ps_mis = _enter(_es8, nc.psum_tensor("ps_mis", [128, 1024], F32))
            with contextlib.ExitStack() as _es12:
                xn_own = _enter(_es12, nc.sbuf_tensor("xn_own", [128, 16, TOWN], BF16))
                wupfm = _enter(_es12, nc.sbuf_tensor("wupfm", [128, 2, 16, 128], BF16))
                upb_t = _enter(_es12, nc.sbuf_tensor("upb_t", [16, 2, 128], BF16))
                t1_sb = _enter(_es12, nc.sbuf_tensor("t1_sb", [16, TOWN], BF16))
                upla_sb = _enter(_es12, nc.sbuf_tensor("upla_sb", [128, 16, R], BF16))
                @block.sync
                def _(sy):
                    if PHASES < 3:
                        return
                    sy.wait_ge(p2, 4)
                    sy.wait_ge(dxn, 16 * NTILES)
                    for tt in range(8):
                        sy.dma_start(out=xn_own[:, :, 128 * tt:128 * (tt + 1)],
                                     in_=xn_hbm.ap()[128 * (NT + tt):128 * (NT + tt + 1), :],
                                     transpose=True).then_inc(dxb, 16)
                    for hc in range(64):
                        sy.wait_ge(dpu_o, 16 * (2 * hc + 2))
                        if hc >= 2:
                            sy.wait_ge(t_y1, 2 * hc - 2)
                        sy.dma_start(out=wupfm[:, hc % 2, :, :],
                                     in_=dequp.ap()[128 * hc:128 * (hc + 1), :],
                                     transpose=True).then_inc(dtru, 16)

                @block.gpsimd
                def _(g):
                    if PHASES < 3:
                        return
                    g.wait_ge(p2, 4)
                    g.dma_start(out=upla_sb[:, :, :],
                                in_=upla.rearrange("(c p) r -> p c r", p=128)).then_inc(dw, 16)
                    for hc in range(64):
                        if hc >= 2:
                            g.wait_ge(t_y1, 2 * hc - 2)
                        g.dma_start(out=upb_t[:, hc % 2, :],
                                    in_=uplb[:, 128 * hc:128 * (hc + 1)]).then_inc(dub, 16)

                @block.tensor
                def _(tn):
                    if PHASES < 3:
                        return
                    tn.wait_ge(p2, 4)
                    tn.wait_ge(dxb, 16 * 8)
                    tn.wait_ge(dw, 16 * 14)
                    for tc in range(2):
                        for dcx in range(16):
                            mm = tn.matmul(ps_mis[0:16, 512 * tc:512 * (tc + 1)],
                                           upla_sb[:, dcx, :],
                                           xn_own[:, dcx, 512 * tc:512 * (tc + 1)],
                                           start=(dcx == 0), stop=(dcx == 15))
                    mm.then_inc(t_t1, 1)
                    tn.wait_ge(s_t1, 1)
                    for hc in range(64):
                        tn.wait_ge(dtru, 16 * (hc + 1))
                        tn.wait_ge(dub, 16 * (hc + 1))
                        if hc >= 2:
                            tn.wait_ge(s_x2, 2 * hc - 2)
                        for tc in range(2):
                            b = (hc % 2) * 2 + tc
                            for dcx in range(16):
                                tn.matmul(ps_y[:, 512 * b:512 * (b + 1)],
                                          wupfm[:, hc % 2, dcx, :],
                                          xn_own[:, dcx, 512 * tc:512 * (tc + 1)],
                                          start=(dcx == 0), stop=False)
                            tn.matmul(ps_y[:, 512 * b:512 * (b + 1)],
                                      upb_t[:, hc % 2, :],
                                      t1_sb[:, 512 * tc:512 * (tc + 1)],
                                      start=False, stop=True).then_inc(t_y1, 1)
                    tn.matmul(ps_dmy[0:1, 0:1], onesc[0:1, 0:1], onesc[0:1, 0:1],
                              start=True, stop=True).then_inc(p3, 1)

                @block.scalar
                def _(s):
                    if PHASES < 3:
                        return
                    s.wait_ge(p2, 4)
                    s.wait_ge(dpd_o, 16 * NDNP)
                    s.wait_ge(t_t1, 1)
                    s.copy(t1_sb[:, 0:512], ps_mis[0:16, 0:512])
                    s.copy(t1_sb[:, 512:1024], ps_mis[0:16, 512:1024]).then_inc(s_t1, 1)
                    for hc in range(64):
                        for tc in range(2):
                            b = (hc % 2) * 2 + tc
                            s.wait_ge(t_y1, 2 * hc + tc + 1)
                            s.activation(x2[:, hc, 512 * tc:512 * (tc + 1)],
                                         ps_y[:, 512 * b:512 * (b + 1)], AF.Relu,
                                         bias=bup_sb[:, hc:hc + 1]).then_inc(s_x2, 1)
                    s.copy(sv1[:, 4:5], rstda[:, 4:5]).then_inc(p3, 1)

            with contextlib.ExitStack() as _es12:
                ctx_fm = _enter(_es12, nc.sbuf_tensor("ctx_fm", [128, 16, TOWN], BF16))
                wo_bf = _enter(_es12, nc.sbuf_tensor("wo_bf", [128, 2, 16, 128], BF16))
                s_sb = _enter(_es12, nc.sbuf_tensor("s_sb", [128, 2, TOWN], BF16))
                @block.sync
                def _(sy):
                    if PHASES < 4:
                        return
                    sy.wait_ge(p3, 2)
                    sy.wait_ge(cc, 1)
                    off = sy.value_load(tokoff_sb[0:1, 0:1])
                    sy.dma_start(out=ctx_fm[:, :, :],
                                 in_=ctx_out_re[:, :, bass.ds(off, TOWN)]).then_inc(dcf, 16)

                @block.gpsimd
                def _(g):
                    if PHASES < 4:
                        return
                    g.wait_ge(p3, 2)
                    for ec in range(16):
                        if ec >= 2:
                            g.wait_ge(t_s, 2 * ec - 2)
                        g.dma_start(
                            out=wo_bf[:, ec % 2, :, :],
                            in_=wo[:, 128 * ec:128 * (ec + 1)].rearrange("(c p) e -> p c e", p=128),
                        ).then_inc(dwo, 16)
                        g.wait_ge(s_sev, ec + 1)
                        g.dma_start(out=s_hbm.ap()[128 * ec:128 * (ec + 1), :],
                                    in_=s_sb[:, ec % 2, :]).then_inc(dse, 16)

                @block.tensor
                def _(tn):
                    if PHASES < 4:
                        return
                    tn.wait_ge(s_x2, 128)
                    for tc in range(2):
                        for hc in range(64):
                            mm = tn.matmul(ps_mis[0:16, 512 * tc:512 * (tc + 1)],
                                           dnla_sb[:, hc, :],
                                           x2[:, hc, 512 * tc:512 * (tc + 1)],
                                           start=(hc == 0), stop=(hc == 63))
                    mm.then_inc(t_t2, 1)
                    tn.wait_ge(dcf, 16)
                    for ec in range(16):
                        tn.wait_ge(dwo, 16 * (ec + 1))
                        if ec >= 2:
                            tn.wait_ge(s_sev, ec - 1)
                        for tc in range(2):
                            b = (ec % 2) * 2 + tc
                            for dcx in range(16):
                                mm = tn.matmul(ps_y[:, 512 * b:512 * (b + 1)],
                                               wo_bf[:, ec % 2, dcx, :],
                                               ctx_fm[:, dcx, 512 * tc:512 * (tc + 1)],
                                               start=(dcx == 0), stop=(dcx == 15))
                                if dcx == 15:
                                    mm.then_inc(t_s, 1)
                    tn.matmul(ps_dmy[0:1, 0:1], onesc[0:1, 0:1], onesc[0:1, 0:1],
                              start=True, stop=True).then_inc(p4, 1)

                @block.scalar
                def _(s):
                    if PHASES < 4:
                        return
                    s.wait_ge(t_t2, 1)
                    s.copy(t2_sb[:, 0:512], ps_mis[0:16, 0:512])
                    s.copy(t2_sb[:, 512:1024], ps_mis[0:16, 512:1024]).then_inc(s_t2, 1)
                    for ec in range(16):
                        if ec >= 2:
                            s.wait_ge(dse, 16 * (ec - 1))
                        for tc in range(2):
                            b = (ec % 2) * 2 + tc
                            s.wait_ge(t_s, 2 * ec + tc + 1)
                            act = s.activation(s_sb[:, ec % 2, 512 * tc:512 * (tc + 1)],
                                               ps_y[:, 512 * b:512 * b + 512],
                                               AF.Identity, bias=bo_sb[:, ec:ec + 1])
                        act.then_inc(s_sev, 1)
                    s.copy(sv1[:, 5:6], rstda[:, 5:6]).then_inc(p4, 1)

            sgta = _enter(_es8, nc.sbuf_tensor("sgta", [128, 16, 8, 128], BF16))
            with contextlib.ExitStack() as _es12:
                wdnfm = _enter(_es12, nc.sbuf_tensor("wdnfm", [128, 2, 32, 128], BF16))
                dnb_t = _enter(_es12, nc.sbuf_tensor("dnb_t", [16, 2, 128], BF16))
                s_t = _enter(_es12, nc.sbuf_tensor("s_t", [128, 2, TOWN], BF16))
                g_sc = _enter(_es12, nc.sbuf_tensor("g_sc", [128, 2, TOWN], F32))
                sg_sb = _enter(_es12, nc.sbuf_tensor("sg_sb", [128, 2, TOWN], BF16))
                @block.sync
                def _(sy):
                    if PHASES < 5:
                        return
                    sy.wait_ge(p4, 2)
                    for dc in range(16):
                        for hh in range(2):
                            hidx = 2 * dc + hh
                            if hidx >= 2:
                                sy.wait_ge(t_half, hidx - 1)
                            sy.dma_start(out=wdnfm[:, hidx % 2, :, :],
                                         in_=deqdn.ap()[128 * dc:128 * (dc + 1),
                                                        4096 * hh:4096 * (hh + 1)],
                                         transpose=True).then_inc(dtrd, 16)
                        if dc >= 2:
                            dd = dc - 2
                            sy.wait_ge(dsg, 16 * (dd + 1))
                            sy.dma_start(out=sgta[:, dd, :, :],
                                         in_=sg_hbm.ap()[128 * dd:128 * (dd + 1), :],
                                         transpose=True).then_inc(dtp, 16)
                    for dd in range(14, 16):
                        sy.wait_ge(dsg, 16 * (dd + 1))
                        sy.dma_start(out=sgta[:, dd, :, :],
                                     in_=sg_hbm.ap()[128 * dd:128 * (dd + 1), :],
                                     transpose=True).then_inc(dtp, 16)

                @block.gpsimd
                def _(g):
                    if PHASES < 5:
                        return
                    g.wait_ge(p4, 2)
                    for dc in range(16):
                        if dc >= 2:
                            g.wait_ge(t_y2, 2 * dc - 2)
                        g.dma_start(out=dnb_t[:, dc % 2, :],
                                    in_=dnlb[:, 128 * dc:128 * (dc + 1)]).then_inc(ddb, 16)
                        g.wait_ge(dse, 16 * (dc + 1))
                        if dc >= 2:
                            g.wait_ge(v_sg, dc - 1)
                        g.dma_start(out=s_t[:, dc % 2, :],
                                    in_=s_hbm.ap()[128 * dc:128 * (dc + 1), :]).then_inc(dst, 16)
                        g.wait_ge(v_sg, dc + 1)
                        g.dma_start(out=sg_hbm.ap()[128 * dc:128 * (dc + 1), :],
                                    in_=sg_sb[:, dc % 2, :]).then_inc(dsg, 16)

                @block.tensor
                def _(tn):
                    if PHASES < 5:
                        return
                    tn.wait_ge(p4, 2)
                    tn.wait_ge(s_t2, 1)
                    for dc in range(16):
                        tn.wait_ge(ddb, 16 * (dc + 1))
                        if dc >= 2:
                            tn.wait_ge(s_g, dc - 1)
                        for hc in range(64):
                            hidx = 2 * dc + hc // 32
                            if hc % 32 == 0:
                                tn.wait_ge(dtrd, 16 * (hidx + 1))
                            for tc in range(2):
                                b = (dc % 2) * 2 + tc
                                mm = tn.matmul(ps_y[:, 512 * b:512 * (b + 1)],
                                               wdnfm[:, hidx % 2, hc % 32, :],
                                               x2[:, hc, 512 * tc:512 * (tc + 1)],
                                               start=(hc == 0), stop=False)
                            if hc % 32 == 31:
                                mm.then_inc(t_half, 1)
                        for tc in range(2):
                            b = (dc % 2) * 2 + tc
                            tn.matmul(ps_y[:, 512 * b:512 * (b + 1)],
                                      dnb_t[:, dc % 2, :],
                                      t2_sb[:, 512 * tc:512 * (tc + 1)],
                                      start=False, stop=True).then_inc(t_y2, 1)
                    tn.matmul(ps_dmy[0:1, 0:1], onesc[0:1, 0:1], onesc[0:1, 0:1],
                              start=True, stop=True).then_inc(p5, 1)

                @block.scalar
                def _(s):
                    if PHASES < 5:
                        return
                    s.wait_ge(p4, 2)
                    for dc in range(16):
                        for tc in range(2):
                            b = (dc % 2) * 2 + tc
                            s.wait_ge(t_y2, 2 * dc + tc + 1)
                            act = s.activation(g_sc[:, dc % 2, 512 * tc:512 * (tc + 1)],
                                               ps_y[:, 512 * b:512 * (b + 1)], AF.Silu,
                                               bias=bdn_sb[:, dc:dc + 1])
                        act.then_inc(s_g, 1)

                @block.vector
                def _(v):
                    if PHASES < 5:
                        return
                    v.wait_ge(p4, 2)
                    for dc in range(16):
                        v.wait_ge(s_g, dc + 1)
                        v.wait_ge(dst, 16 * (dc + 1))
                        if dc >= 2:
                            v.wait_ge(dsg, 16 * (dc - 1))
                        v.tensor_mul(sg_sb[:, dc % 2, :], g_sc[:, dc % 2, :],
                                     s_t[:, dc % 2, :]).then_inc(v_sg, 1)
                    v.tensor_copy(sv1[:, 6:7], rstda[:, 6:7]).then_inc(p5, 1)

            with contextlib.ExitStack() as _es12:
                xo = _enter(_es12, nc.sbuf_tensor("xo", [128, 2, 2048], F32))
                out_sb = _enter(_es12, nc.sbuf_tensor("out_sb", [128, 2, 2048], F32))
                @block.sync
                def _(sy):
                    if PHASES < 6:
                        return
                    sy.wait_ge(p5, 2)
                    for i in range(8):
                        sy.wait_ge(v_out, i + 1)
                        sy.dma_start(out=out[128 * i:128 * (i + 1), :],
                                     in_=out_sb[:, i % 2, :]).then_inc(dout, 16)
                    sy.wait_ge(dout, 16 * 8)

                @block.gpsimd
                def _(g):
                    if PHASES < 6:
                        return
                    g.wait_ge(p5, 2)
                    for i in range(8):
                        if i >= 2:
                            g.wait_ge(v_out, i - 1)
                        g.dma_start(out=xo[:, i % 2, :],
                                    in_=x_own[128 * i:128 * (i + 1), :]).then_inc(dxo, 16)

                @block.vector
                def _(v):
                    if PHASES < 6:
                        return
                    v.wait_ge(dtp, 16 * 16)
                    for i in range(8):
                        v.wait_ge(dxo, 16 * (i + 1))
                        if i >= 2:
                            v.wait_ge(dout, 16 * (i - 1))
                        v.tensor_add(
                            out_sb[:, i % 2, :].rearrange("p (c j) -> p c j", j=128),
                            sgta[:, :, i, :],
                            xo[:, i % 2, :].rearrange("p (c j) -> p c j", j=128),
                        ).then_inc(v_out, 1)

    return nc


_CACHE = {}


def kernel(**inputs):
    x = np.asarray(inputs["L0_attn_x"])
    Wq = np.asarray(inputs["L0_attn_Wq"])
    Wk = np.asarray(inputs["L0_attn_Wk"])
    Wv = np.asarray(inputs["L0_attn_Wv"])
    Wo = np.asarray(inputs["L0_attn_Wo"])
    bo = np.asarray(inputs["L0_attn_bo"])
    w_up_q = np.asarray(inputs["L0_ffn_w_up_q"])
    w_up_s = np.asarray(inputs["L0_ffn_w_up_scale"])
    b_up = np.asarray(inputs["L0_ffn_b_up"])
    up_a = np.asarray(inputs["L0_ffn_w_up_lora_a"])
    up_b = np.asarray(inputs["L0_ffn_w_up_lora_b"])
    w_dn_q = np.asarray(inputs["L0_ffn_w_down_q"])
    w_dn_s = np.asarray(inputs["L0_ffn_w_down_scale"])
    b_dn = np.asarray(inputs["L0_ffn_b_down"])
    dn_a = np.asarray(inputs["L0_ffn_w_down_lora_a"])
    dn_b = np.asarray(inputs["L0_ffn_w_down_lora_b"])

    x2d = np.ascontiguousarray(x.reshape(SEQ, D).astype(np.float32))

    if "nc" not in _CACHE:
        _CACHE["nc"] = build_nc()
    nc = _CACHE["nc"]

    iq = np.arange(512)
    masks = np.zeros((128, 4, 512), dtype=ml_dtypes.bfloat16)
    for r in range(4):
        masks[:, r, :] = (128 * r + np.arange(128)[:, None] <= iq[None, :])
    masks = np.ascontiguousarray(masks.reshape(128, 4 * 512))
    onesc = np.ones((128, 1), dtype=ml_dtypes.bfloat16)
    onesr = np.ones((1, 128), dtype=np.float32)

    bo_t = np.ascontiguousarray(bo.reshape(16, 128).T).astype(np.float32)
    bup_t = np.ascontiguousarray(b_up.reshape(64, 128).T).astype(np.float32)
    bdn_t = np.ascontiguousarray(b_dn.reshape(16, 128).T).astype(np.float32)

    in_maps = []
    for c in range(NCORE):
        gidx = c // 2
        im = {
            "x_full": x2d,
            "x_own": np.ascontiguousarray(x2d[TOWN * c:TOWN * (c + 1)]),
            "wq_sl": np.ascontiguousarray(Wq[:, 256 * c:256 * (c + 1)]),
            "wk_sl": np.ascontiguousarray(Wk[:, 128 * gidx:128 * (gidx + 1)]),
            "wv_sl": np.ascontiguousarray(Wv[:, 128 * gidx:128 * (gidx + 1)]),
            "wo": Wo,
            "bo_t": bo_t,
            "wupq": w_up_q,
            "wups": w_up_s,
            "bup_t": bup_t,
            "upla": up_a,
            "uplb": up_b,
            "wdnq": w_dn_q,
            "wdns": w_dn_s,
            "bdn_t": bdn_t,
            "dnla": dn_a,
            "dnlb": dn_b,
            "masks_in": masks,
            "onesc_in": onesc,
            "onesr_in": onesr,
            "tokoff_in": np.array([[TOWN * c]], dtype=np.int32),
        }
        in_maps.append(im)

    res = run_bass_kernel_spmd(nc, in_maps, core_ids=list(range(NCORE)))
    outp = np.empty((1, SEQ, D), dtype=np.float32)
    for c in range(NCORE):
        outp[0, TOWN * c:TOWN * (c + 1), :] = res.results[c]["out"]
    return outp
